# version 01 (final)
# baseline (speedup 1.0000x reference)
import numpy as np
import ml_dtypes
from contextlib import ExitStack
import concourse.bass as bass
import concourse.mybir as mybir
from concourse.bass_utils import run_bass_kernel_spmd

F32 = mybir.dt.float32
BF16 = mybir.dt.bfloat16
AF = mybir.ActivationFunctionType
ALU = mybir.AluOpType
AX = mybir.AxisListType
BF = ml_dtypes.bfloat16

N_INNER = 6
T = 2048
D = 1024
HD = 64
NPAIR = 8
DI = 2730
NMT = 22
LAST_M = DI - 21 * 128
TH = T // 2
NEG_BIG = -1.0e38
DEBUG_TAPS = 0
DBG_OUTER = 0
DBG_INNER = 0
TRUNC = 99



def _host_tables():
    half = np.arange(0, HD, 2, dtype=np.float32)
    inv_freq = (1.0 / (10000.0 ** (half / np.float32(HD)))).astype(np.float32)
    t = np.arange(T, dtype=np.float32)
    freqs = (t[:, None] * inv_freq[None, :]).astype(np.float32)
    freqs = np.concatenate([freqs, freqs], axis=1)
    cos = np.cos(freqs).astype(np.float32)
    sin = np.sin(freqs).astype(np.float32)
    base_scale = ((half + np.float32(0.4 * HD)) / np.float32(1.4 * HD)).astype(np.float32)
    power = ((t - T // 2) / np.float32(128.0)).astype(np.float32)
    scale = (base_scale[None, :] ** power[:, None]).astype(np.float32)
    scale = np.concatenate([scale, scale], axis=1)
    inv = (np.float32(1.0) / scale).astype(np.float32)
    out = np.stack([(cos * scale).T, (sin * scale).T, (cos * inv).T, (sin * inv).T])
    out = out.astype(np.float32)
    return np.ascontiguousarray(np.concatenate([out, out], axis=1))


def _host_masks():
    il = np.arange(128)[:, None]
    c = np.arange(384)[None, :]
    valid = (c >= il) & (c <= il + 256)
    mi = np.where(valid, 0.0, NEG_BIG).astype(np.float32)
    maskt = np.zeros((128, 384), np.float32)
    for cc in range(3):
        maskt[:, cc * 128:(cc + 1) * 128] = mi[:, cc * 128:(cc + 1) * 128].T
    return mi, maskt


def _host_consts():
    idf = np.eye(128, dtype=np.float32)
    idb = np.eye(128, dtype=BF)
    bblk = np.zeros((128, 128), np.float32)
    bblk[0:64, 0:64] = 1.0
    bblk[64:128, 64:128] = 1.0
    prot = np.zeros((128, 128), np.float32)
    for base in (0, 64):
        for m in range(32):
            prot[base + m + 32, base + m] = -1.0
            prot[base + m, base + m + 32] = 1.0
    ltrilt = np.tril(np.ones((128, 128), np.float32)).T.copy()
    onesr = np.ones((1, 128), np.float32)
    return idf, idb, bblk.astype(BF), prot.astype(BF), ltrilt, onesr


def _host_packs(inputs):
    ba = np.asarray(inputs["L0_attn_b_attn"], np.float32)
    bp = np.asarray(inputs["L0_attn_b_proj"], np.float32)
    pko = np.concatenate([
        ba[0:1024].reshape(8, 128).T, ba[1024:2048].reshape(8, 128).T,
        ba[2048:3072].reshape(8, 128).T, bp.reshape(8, 128).T], axis=1)
    qs = np.asarray(inputs["L0_ffn_q_scale"], np.float32)
    ks = np.asarray(inputs["L0_ffn_k_scale"], np.float32)
    lng = np.asarray(inputs["L0_ffn_ln_g"], np.float32)
    lnb = np.asarray(inputs["L0_ffn_ln_b"], np.float32)
    b1 = np.asarray(inputs["L0_ffn_b1"], np.float32)
    b2 = np.asarray(inputs["L0_ffn_b2"], np.float32)
    packs = []
    for l in range(6):
        b1a = np.zeros((NMT * 128,), np.float32)
        b1a[0:DI] = b1[l, 0:DI]
        b1g = np.zeros((NMT * 128,), np.float32)
        b1g[0:DI] = b1[l, DI:2 * DI]
        packs.append(np.concatenate([
            np.tile(qs[l], 2)[:, None], np.tile(ks[l], 2)[:, None],
            lng[l].reshape(8, 128).T, lnb[l].reshape(8, 128).T,
            b2[l].reshape(8, 128).T,
            b1a.reshape(NMT, 128).T, b1g.reshape(NMT, 128).T], axis=1))
    return np.ascontiguousarray(pko.astype(np.float32)), \
        np.ascontiguousarray(np.stack(packs).astype(np.float32))



ENGS = ("tensor", "vector", "scalar", "gpsimd", "sync")


SEMS = ("pe", "ve", "se", "ge", "dw", "dx", "dm")


class Prog:
    def __init__(self):
        self.streams = {e: [] for e in ENGS}
        self.cnt = {s: 0 for s in SEMS}

    def n(self, name):
        return self.cnt[name]

    def w(self, *names):
        return [(s, self.cnt[s]) for s in names if self.cnt[s] > 0]

    def op(self, eng, fn, waits=(), inc=None, amt=1):
        if inc is not None:
            self.cnt[inc] += amt
        waits = tuple((s, v) for (s, v) in waits if v > 0)
        self.streams[eng].append((fn, waits, inc, amt))

    def dma(self, eng, out_ap, in_ap, waits=(), inc="dx"):
        self.op(eng, lambda e, o=out_ap, i=in_ap: e.dma_start(o, i),
                waits=waits, inc=inc, amt=16)

    def runners(self, sems):
        def mk(stream):
            def run(e):
                for fn, waits, inc, amt in stream:
                    for s, v in waits:
                        e.wait_ge(sems[s], v)
                    r = fn(e)
                    if inc is not None:
                        r.then_inc(sems[inc], amt)

            return run
        return {e: mk(self.streams[e]) for e in ENGS}


class NS:
    pass



def build_program():
    nc = bass.Bass("TRN2", target_bir_lowering=False)
    s = NS()
    s.nc = nc
    s.x_in = nc.dram_tensor("x", [T, D], F32, kind="ExternalInput").ap()
    s.y_out = nc.dram_tensor("y", [T, D], F32, kind="ExternalOutput").ap()
    s.w_attn = nc.dram_tensor("w_attn", [D, 3 * D], F32, kind="ExternalInput").ap()
    s.w_proj = nc.dram_tensor("w_proj", [D, D], F32, kind="ExternalInput").ap()
    s.wqkv = nc.dram_tensor("wqkv", [6, D, 3 * D], F32, kind="ExternalInput").ap()
    s.wo = nc.dram_tensor("wo", [6, D, D], F32, kind="ExternalInput").ap()
    s.w1 = nc.dram_tensor("w1", [6, D, 2 * DI], F32, kind="ExternalInput").ap()
    s.w2 = nc.dram_tensor("w2", [6, DI, D], F32, kind="ExternalInput").ap()
    s.tabs_in = nc.dram_tensor("tabs", [4, 128, T], F32, kind="ExternalInput").ap()
    s.maski_in = nc.dram_tensor("maski", [128, 384], F32, kind="ExternalInput").ap()
    s.maskt_in = nc.dram_tensor("maskt", [128, 384], F32, kind="ExternalInput").ap()
    s.nidf_in = nc.dram_tensor("nidf", [128, 128], F32, kind="ExternalInput").ap()
    s.idf_in = nc.dram_tensor("idf", [128, 128], F32, kind="ExternalInput").ap()
    s.idb_in = nc.dram_tensor("idb", [128, 128], BF16, kind="ExternalInput").ap()
    s.bblk_in = nc.dram_tensor("bblk", [128, 128], BF16, kind="ExternalInput").ap()
    s.prot_in = nc.dram_tensor("prot", [128, 128], BF16, kind="ExternalInput").ap()
    s.ltrilt_in = nc.dram_tensor("ltrilt", [128, 128], F32, kind="ExternalInput").ap()
    s.onesr_in = nc.dram_tensor("onesr", [1, 128], F32, kind="ExternalInput").ap()
    s.pko_in = nc.dram_tensor("pko", [128, 32], F32, kind="ExternalInput").ap()
    s.packl_in = nc.dram_tensor("packl", [6, 128, 70], F32, kind="ExternalInput").ap()
    s.uscr = nc.dram_tensor("uscr", [128, 8 * T], F32).ap()
    nslots = DEBUG_TAPS + (7 if DBG_OUTER else 0) + (5 if DBG_INNER else 0)
    if nslots:
        s.dbg = nc.dram_tensor("dbg", [nslots, 128, 8 * T], F32,
                               kind="ExternalOutput").ap()

    P = Prog()
    with ExitStack() as ctx:
        E = ctx.enter_context
        s.X = E(nc.sbuf_tensor("X", [128, 8 * T], F32))
        s.MASKI = E(nc.sbuf_tensor("MASKI", [128, 384], F32))
        s.MASKT = E(nc.sbuf_tensor("MASKT", [128, 384], F32))
        s.NIDF = E(nc.sbuf_tensor("NIDF", [128, 128], F32))
        s.IDF = E(nc.sbuf_tensor("IDF", [128, 128], F32))
        s.IDB = E(nc.sbuf_tensor("IDB", [128, 128], BF16))
        s.BBLK = E(nc.sbuf_tensor("BBLK", [128, 128], BF16))
        s.PROT = E(nc.sbuf_tensor("PROT", [128, 128], BF16))
        s.LTRILT = E(nc.sbuf_tensor("LTRILT", [128, 128], F32))
        s.ONESR = E(nc.sbuf_tensor("ONESR", [1, 128], F32))
        s.ONESC = E(nc.sbuf_tensor("ONESC", [128, 1], BF16))
        s.PKO = E(nc.sbuf_tensor("PKO", [128, 32], F32))
        s.NPKO = E(nc.sbuf_tensor("NPKO", [128, 16], F32))
        s.PACK = E(nc.sbuf_tensor("PACK", [128, 70], F32))
        s.TAB = E(nc.sbuf_tensor("TAB", [128, 4 * T], F32))
        s.AR = E(nc.sbuf_tensor("AR", [128, 51200], BF16))
        s.PSF = E(nc.psum_tensor("PSF", [128, 4096], F32))
        sems = {k: E(nc.semaphore(k)) for k in SEMS}

        s.arf = s.AR[:, :].bitcast(F32)
        s.arb = lambda c0, n: s.AR[:, c0:c0 + n]
        s.arf32 = lambda c0, n: s.arf[:, c0 // 2: c0 // 2 + n]
        s.PA = s.PSF[:, 0:1024]
        s.PB = s.PSF[:, 1024:2048]
        s.PC = s.PSF[:, 2048:3072]
        s.PD = s.PSF[:, 3072:4096]

        build_prologue(P, s)
        if DBG_OUTER:
            P.dma("sync", s.dbg[0], s.X[:, :], waits=P.w("ve", "pe"))
        build_outer(P, s)
        if DEBUG_TAPS:
            P.dma("sync", s.dbg[0], s.X[:, :], waits=P.w("ve", "pe", "se", "dx"))
        for l in range(N_INNER):
            build_inner_attn(P, s, l)
            if TRUNC < 99:
                continue
            build_inner_ffn(P, s, l)
            if DEBUG_TAPS and l + 1 < DEBUG_TAPS:
                P.dma("sync", s.dbg[l + 1], s.X[:, :], waits=P.w("ve", "pe", "se"))
        build_final(P, s)
        P.op("sync", lambda e: None, waits=P.w("dx", "dw", "dm"))

        r = P.runners(sems)
        with nc.Block() as blk:
            blk.tensor(r["tensor"])
            blk.vector(r["vector"])
            blk.scalar(r["scalar"])
            blk.gpsimd(r["gpsimd"])
            blk.sync(r["sync"])
    n_instr = sum(len(v) for v in P.streams.values())
    print(f"[kernel] built: {n_instr} ops "
          f"({', '.join(f'{k}:{len(v)}' for k, v in P.streams.items())})")
    return nc



def build_prologue(P, s):
    for dst, src in ((s.IDF[:], s.idf_in[:, :]), (s.IDB[:], s.idb_in[:, :]),
                     (s.BBLK[:], s.bblk_in[:, :]), (s.PROT[:], s.prot_in[:, :]),
                     (s.LTRILT[:], s.ltrilt_in[:, :]), (s.ONESR[:], s.onesr_in[:, :]),
                     (s.PKO[:], s.pko_in[:, :]), (s.MASKI[:], s.maski_in[:, :]),
                     (s.MASKT[:], s.maskt_in[:, :]), (s.NIDF[:], s.nidf_in[:, :])):
        P.dma("sync", dst, src)
    for i in range(4):
        P.dma("sync", s.TAB[:, i * T:(i + 1) * T], s.tabs_in[i])
    P.op("vector", lambda e: e.memset(s.ONESC[:], 1.0), inc="ve")
    P.op("vector", lambda e: e.tensor_scalar_mul(s.NPKO[:], s.PKO[:, 0:16], -1.0),
         waits=P.w("dx"), inc="ve")

    XTF = s.arf32(0, 4096)
    for tc in range(4):
        src = s.x_in.rearrange("(tc tt p) d -> tc p tt d", tc=4, p=128)[tc]
        P.dma("sync", XTF[:, :].rearrange("p (tt d) -> p tt d", tt=4), src,
              waits=P.w("ve") if tc > 0 else ())
        wd = P.w("dx")
        for tt in range(4):
            for dt in range(8):
                par = (tt * 8 + dt) % 2
                ps = s.PA[:, 0:128] if par == 0 else s.PB[:, 0:128]
                P.op("tensor", lambda e, tt=tt, dt=dt, ps=ps: e.transpose(
                    ps, XTF[:, tt * 1024 + dt * 128: tt * 1024 + dt * 128 + 128],
                    s.IDF[:]), waits=(wd if (tt == 0 and dt < 2) else P.w("ve")),
                    inc="pe")
                col = dt * 2048 + tc * 512 + tt * 128
                P.op("vector", lambda e, col=col, ps=ps: e.tensor_copy(
                    s.X[:, col:col + 128], ps), waits=P.w("pe"), inc="ve")



def emit_feature_ln(P, s, src_shift, affine, out_xn, sq_c0=20480, stat_c0=48128):
    X, PACK, ONESR, ONESC, PC, PA, PB = s.X, s.PACK, s.ONESR, s.ONESC, s.PC, s.PA, s.PB
    XSB = s.arb(0, 16384)
    SQ = [s.arb(sq_c0, 1024), s.arb(sq_c0 + 1024, 1024)]
    ST = s.arf32(stat_c0, 1024)
    SF1 = s.arf32(36864, 512)

    for dt in range(8):
        dst = XSB[:, dt * 2048:(dt + 1) * 2048]
        w0 = P.w("pe") if dt == 0 else ()
        if src_shift and dt >= 4:
            P.op("vector", lambda e, dt=dt, dst=dst: e.tensor_copy(
                dst[:, 1:2048], X[:, dt * 2048: dt * 2048 + 2047]), waits=w0, inc="ve")
            P.op("vector", lambda e, dst=dst: e.memset(dst[:, 0:1], 0.0), inc="ve")
        else:
            P.op("vector", lambda e, dt=dt, dst=dst: e.tensor_copy(
                dst, X[:, dt * 2048:(dt + 1) * 2048]), waits=w0, inc="ve")
    wxb = P.w("ve", "ge")

    for tb in range(4):
        for dt in range(8):
            P.op("tensor", lambda e, dt=dt, tb=tb: e.matmul(
                PC[0:1, 0:512], ONESC[:],
                XSB[:, dt * 2048 + tb * 512: dt * 2048 + tb * 512 + 512],
                start=(dt == 0), stop=(dt == 7)),
                waits=(wxb + P.w("ve") if dt == 0 else ()),
                inc="pe" if dt == 7 else None)
        for dt in range(8):
            sq = SQ[dt % 2]
            P.op("scalar", lambda e, dt=dt, tb=tb, sq=sq: e.activation(
                sq[:, 0:512], XSB[:, dt * 2048 + tb * 512: dt * 2048 + tb * 512 + 512],
                AF.Square), waits=(wxb if dt < 2 else P.w("pe")), inc="se")
            P.op("tensor", lambda e, dt=dt, sq=sq: e.matmul(
                PC[0:1, 512:1024], ONESC[:], sq[:, 0:512],
                start=(dt == 0), stop=(dt == 7)), waits=P.w("se"), inc="pe")
        wmm = P.w("pe")
        P.op("vector", lambda e: e.tensor_scalar_mul(ST[0:1, 0:512], PC[0:1, 0:512],
                                                     1.0 / D), waits=wmm, inc="ve")
        P.op("vector", lambda e: e.tensor_scalar_mul(ST[0:1, 512:1024],
                                                     PC[0:1, 512:1024], 1.0 / D), inc="ve")
        P.op("tensor", lambda e, tb=tb: e.matmul(PA[:, (tb % 2) * 512:(tb % 2) * 512 + 512],
                                                 ONESR[:], ST[0:1, 0:512],
                                                 start=True, stop=True),
             waits=P.w("ve"), inc="pe")
        P.op("vector", lambda e: e.tensor_mul(SF1[0:1, 0:512], ST[0:1, 0:512],
                                              ST[0:1, 0:512]), inc="ve")
        P.op("vector", lambda e: e.tensor_sub(ST[0:1, 0:512], ST[0:1, 512:1024],
                                              SF1[0:1, 0:512]), waits=P.w("pe"),
             inc="ve")
        P.op("vector", lambda e: e.tensor_scalar_add(ST[0:1, 0:512], ST[0:1, 0:512],
                                                     1e-5), inc="ve")
        P.op("scalar", lambda e: e.activation(ST[0:1, 512:1024], ST[0:1, 0:512],
                                              AF.Ln), waits=P.w("ve"), inc="se")
        P.op("scalar", lambda e: e.activation(ST[0:1, 0:512], ST[0:1, 512:1024],
                                              AF.Exp, scale=-0.5), inc="se")
        P.op("vector", lambda e: e.tensor_copy(SF1[0:1, 0:512], ST[0:1, 0:512]),
             waits=P.w("se"), inc="ve")
        P.op("tensor", lambda e, tb=tb: e.matmul(PB[:, (tb % 2) * 512:(tb % 2) * 512 + 512],
                                                 ONESR[:], ST[0:1, 0:512],
                                                 start=True, stop=True),
             waits=P.w("ve"), inc="pe")
        wbc = P.w("pe")
        for dt in range(8):
            pa = PA[:, (tb % 2) * 512:(tb % 2) * 512 + 512]
            pb = PB[:, (tb % 2) * 512:(tb % 2) * 512 + 512]
            shifted = src_shift and dt >= 4
            c0 = dt * 2048 + tb * 512
            SFa = s.arf32(36864, 1024)
            SFb = s.arf32(38912, 1024)
            if shifted and tb == 0:
                P.op("vector", lambda e, pa=pa: e.tensor_scalar_mul(
                    SFa[:, 0:1], pa[:, 0:1], -1.0), waits=wbc, inc="ve")
                P.op("vector", lambda e, c0=c0, pa=pa: e.tensor_sub(
                    SFa[:, 1:512], X[:, c0:c0 + 511], pa[:, 1:512]), waits=wbc, inc="ve")
            elif shifted:
                P.op("vector", lambda e, c0=c0, pa=pa: e.tensor_sub(
                    SFa[:, 0:512], X[:, c0 - 1:c0 + 511], pa[:, 0:512]), waits=wbc, inc="ve")
            else:
                P.op("vector", lambda e, c0=c0, pa=pa: e.tensor_sub(
                    SFa[:, 0:512], X[:, c0:c0 + 512], pa[:, 0:512]), waits=wbc, inc="ve")
            if out_xn is None:
                P.op("vector", lambda e, c0=c0, pb=pb: e.tensor_mul(
                    X[:, c0:c0 + 512], SFa[:, 0:512], pb[:, 0:512]), inc="ve")
            else:
                g_base, b_base = affine
                P.op("vector", lambda e, pb=pb: e.tensor_mul(
                    SFb[:, 0:512], SFa[:, 0:512], pb[:, 0:512]), inc="ve")
                P.op("vector", lambda e, dt=dt, tb=tb, g=g_base, b=b_base: e.tensor_scalar(
                    s.arb(out_xn + dt * 2048, 2048)[:, tb * 512:(tb + 1) * 512],
                    SFb[:, 0:512], PACK[:, g + dt:g + dt + 1],
                    PACK[:, b + dt:b + dt + 1], ALU.mult, ALU.add),
                    waits=P.w("dx"), inc="ve")


def build_outer(P, s):
    X, AR, PA, PB, PC, PD, PSF = s.X, s.AR, s.PA, s.PB, s.PC, s.PD, s.PSF
    arb, arf32 = s.arb, s.arf32
    XB = arb(0, 16384)
    Y = arb(16384, 16384)
    QF = arb(32768, 2048)
    KF = arb(34816, 2048)
    KT = arb(36864, 2048)
    VT1 = arb(38912, 2080)
    FM1 = arf32(40992, 2048)
    FM2 = arf32(45088, 2048)
    WQS = arb(49184, 1024)
    ATm = arb(50208, 128)
    YSBF = arf32(50336, 128)
    KVS = arb(50592, 160)
    KVF = arf32(50752, 130)
    DEN = arf32(51012, 16)
    RDEN = arf32(51044, 16)
    WOS = arb(49184, 1024)

    def vt1(tile, h):
        return VT1[:, tile * 130 + h * 65: tile * 130 + (h + 1) * 65]

    for dt in range(8):
        P.op("vector", lambda e, dt=dt: e.tensor_copy(
            XB[:, dt * 2048:(dt + 1) * 2048], X[:, dt * 2048:(dt + 1) * 2048]),
            inc="ve")
    for tile in range(16):
        for h in range(2):
            c = tile * 130 + h * 65 + 64
            P.op("vector", lambda e, c=c: e.memset(VT1[:, c:c + 1], 1.0), inc="ve")

    def fullmm(slab, wfirst):
        for tb in range(4):
            for dt in range(8):
                P.op("tensor", lambda e, tb=tb, dt=dt, slab=slab: e.matmul(
                    PSF[:, tb * 512:(tb + 1) * 512],
                    slab[:, dt * 128:(dt + 1) * 128],
                    XB[:, dt * 2048 + tb * 512: dt * 2048 + tb * 512 + 512],
                    start=(dt == 0), stop=(dt == 7)),
                    waits=(wfirst if (tb == 0 and dt == 0) else ()),
                    inc="pe" if (tb == 3 and dt == 7) else None)

    def slab_load(which, p):
        c0 = which * 1024 + p * 128
        src = s.w_attn[:, c0:c0 + 128].rearrange("(dt pp) c -> pp dt c", pp=128)
        P.dma("gpsimd", WQS[:, :].rearrange("p (dt c) -> p dt c", dt=8),
              src, waits=P.w("pe"), inc="dw")
        return [("dw", P.n("dw"))]

    for p in range(NPAIR):
        wsl = slab_load(0, p)
        fullmm(WQS, wsl + P.w("ve", "se"))
        wq = P.w("pe")
        P.op("scalar", lambda e, p=p: e.activation(FM1[:, :], PSF[:, 0:2048], AF.Relu,
                                                   bias=s.PKO[:, p:p + 1]),
             waits=wq + P.w("ve"), inc="se")
        P.op("scalar", lambda e, p=p: e.activation(FM2[:, :], PSF[:, 0:2048], AF.Relu,
                                                   bias=s.NPKO[:, p:p + 1], scale=-1.0),
             waits=wq, inc="se")
        P.op("scalar", lambda e: e.activation(FM2[:, :], FM2[:, :], AF.Exp,
                                              scale=-1.0), inc="se")
        P.op("vector", lambda e: e.tensor_add(QF[:, :], FM1[:, :], FM2[:, :]),
             waits=P.w("se"), inc="ve")

        wsl = slab_load(1, p)
        fullmm(WQS, wsl + P.w("ve", "se"))
        wk = P.w("pe")
        P.op("scalar", lambda e, p=p: e.activation(FM1[:, :], PSF[:, 0:2048], AF.Relu,
                                                   bias=s.PKO[:, 8 + p:9 + p]),
             waits=wk + P.w("ve"), inc="se")
        P.op("scalar", lambda e, p=p: e.activation(FM2[:, :], PSF[:, 0:2048], AF.Relu,
                                                   bias=s.NPKO[:, 8 + p:9 + p], scale=-1.0),
             waits=wk, inc="se")
        P.op("scalar", lambda e: e.activation(FM2[:, :], FM2[:, :], AF.Exp,
                                              scale=-1.0), inc="se")
        P.op("vector", lambda e: e.tensor_add(FM2[:, :], FM1[:, :], FM2[:, :]),
             waits=P.w("se"), inc="ve")
        P.op("vector", lambda e: e.tensor_copy(KF[:, :], FM2[:, :]), inc="ve")
        for tt in range(16):
            pd = PD[:, 0:128] if tt % 2 == 0 else PD[:, 128:256]
            P.op("tensor", lambda e, tt=tt, pd=pd: e.transpose(
                pd, FM2[:, tt * 128:(tt + 1) * 128], s.IDF[:]),
                waits=P.w("ve"), inc="pe")
            P.op("vector", lambda e, tt=tt, pd=pd: e.tensor_copy(
                KT[:, tt * 128:(tt + 1) * 128], pd), waits=P.w("pe"), inc="ve")

        wsl = slab_load(2, p)
        fullmm(WQS, wsl + P.w("ve", "se"))
        wv = P.w("pe")
        P.op("vector", lambda e, p=p: e.tensor_scalar_add(FM1[:, :], PSF[:, 0:2048],
                                                          s.PKO[:, 16 + p:17 + p]),
             waits=wv + P.w("ve", "se"), inc="ve")
        for tt in range(16):
            pd = PD[:, 0:128] if tt % 2 == 0 else PD[:, 128:256]
            P.op("tensor", lambda e, tt=tt, pd=pd: e.transpose(
                pd, FM1[:, tt * 128:(tt + 1) * 128], s.IDF[:]),
                waits=P.w("ve"), inc="pe")
            P.op("vector", lambda e, tt=tt, pd=pd: e.tensor_copy(
                vt1(tt, 0)[:, 0:64], pd[:, 0:64]), waits=P.w("pe"), inc="ve")
            P.op("vector", lambda e, tt=tt, pd=pd: e.tensor_copy(
                vt1(tt, 1)[:, 0:64], pd[:, 64:128]), inc="ve")

        if DBG_OUTER and p == 0:
            P.dma("gpsimd", s.dbg[1][:, 0:2048], QF[:, :], waits=P.w("ve"), inc="dw")
            P.dma("gpsimd", s.dbg[1][:, 2048:4096], KF[:, :], waits=P.w("ve"), inc="dw")
            P.dma("gpsimd", s.dbg[1][:, 4096:6176], VT1[:, :], waits=P.w("ve"), inc="dw")
            P.dma("gpsimd", s.dbg[1][:, 6400:8448], KT[:, :], waits=P.w("ve"), inc="dw")

        P.op("vector", lambda e: e.memset(KVF[0:64, :], 0.0), inc="ve")
        for ch in range(16):
            t0 = ch * 128
            for h in range(2):
                hb = h * 64
                nreg = PC[:, 512 + h * 128: 512 + h * 128 + 65]
                kreg = PD[0:64, 512 + h * 128: 512 + h * 128 + 65]
                P.op("tensor", lambda e, t0=t0, hb=hb: e.matmul(
                    PC[:, 0:128], KF[hb:hb + 64, t0:t0 + 128],
                    QF[hb:hb + 64, t0:t0 + 128], start=True, stop=True),
                    waits=P.w("ve"), inc="pe")
                P.op("vector", lambda e: e.tensor_mul(ATm[:, :], PC[:, 0:128],
                                                      s.LTRILT[:]),
                     waits=P.w("pe"), inc="ve")
                P.op("tensor", lambda e, ch=ch, t0=t0, h=h, nreg=nreg: e.matmul(
                    nreg, ATm[:, :], vt1(t0 // 128, h),
                    start=True, stop=(ch == 0)), waits=P.w("ve"),
                    inc="pe" if ch == 0 else None)
                if ch > 0:
                    P.op("tensor", lambda e, t0=t0, hb=hb, h=h, nreg=nreg: e.matmul(
                        nreg, QF[hb:hb + 64, t0:t0 + 128],
                        KVS[hb:hb + 64, h * 65:(h + 1) * 65],
                        start=False, stop=True), inc="pe")
                P.op("tensor", lambda e, t0=t0, hb=hb, h=h, kreg=kreg: e.matmul(
                    kreg, KT[:, t0 + hb:t0 + hb + 64], vt1(t0 // 128, h),
                    start=True, stop=True), inc="pe")
            P.op("vector", lambda e: e.tensor_add(
                KVF[0:64, 0:65], KVF[0:64, 0:65], PD[0:64, 512:577]),
                waits=P.w("pe"), inc="ve")
            P.op("vector", lambda e: e.tensor_add(
                KVF[0:64, 65:130], KVF[0:64, 65:130], PD[0:64, 640:705]), inc="ve")
            P.op("vector", lambda e: e.tensor_copy(KVS[0:64, 0:130], KVF[0:64, :]),
                 inc="ve")
            P.op("vector", lambda e: e.tensor_copy(KVS[64:128, 0:130], KVF[0:64, :]),
                 inc="ve")
            P.op("vector", lambda e: e.tensor_scalar_add(
                DEN[:, 0:1], PC[:, 576:577], 1e-6), inc="ve")
            P.op("vector", lambda e: e.tensor_scalar_add(
                DEN[:, 1:2], PC[:, 704:705], 1e-6), inc="ve")
            P.op("vector", lambda e: e.reciprocal(RDEN[:, 0:1], DEN[:, 0:1]), inc="ve")
            P.op("vector", lambda e: e.reciprocal(RDEN[:, 1:2], DEN[:, 1:2]), inc="ve")
            for h in range(2):
                P.op("vector", lambda e, h=h: e.tensor_single_scalar(
                    YSBF[:, h * 64:(h + 1) * 64],
                    PC[:, 512 + h * 128: 512 + h * 128 + 64],
                    RDEN[:, h:h + 1], ALU.mult), waits=P.w("ve"), inc="ve")
            if DBG_OUTER and p == 0 and ch in (0, 1):
                sl = 5 + ch
                P.dma("gpsimd", s.dbg[sl][:, 0:128], ATm[:, :], waits=P.w("ve"), inc="dw")
                P.dma("gpsimd", s.dbg[sl][:, 256:416], KVS[:, 0:160], waits=P.w("ve"), inc="dw")
                P.dma("sync", s.dbg[sl][:, 512:640], YSBF[:, :], waits=P.w("ve"))
                P.dma("sync", s.dbg[sl][:, 768:784], DEN[:, 0:16], waits=P.w("ve"))
                P.dma("sync", s.dbg[sl][:, 1024:1040], RDEN[:, 0:16], waits=P.w("ve"))
            P.op("tensor", lambda e: e.transpose(PD[:, 256:384], YSBF[:, :], s.IDF[:]),
                 waits=P.w("ve") + ([("dw", P.n("dw")), ("dx", P.n("dx"))]
                                    if (DBG_OUTER and p == 0 and ch in (0, 1)) else []),
                 inc="pe")
            P.op("vector", lambda e, p=p, t0=t0: e.tensor_copy(
                Y[:, p * 2048 + t0: p * 2048 + t0 + 128], PD[:, 256:384]),
                waits=P.w("pe"), inc="ve")

    if DBG_OUTER:
        P.dma("gpsimd", s.dbg[2], Y[:, :], waits=P.w("ve", "pe"), inc="dw")
    for oc in range(8):
        src = s.w_proj[:, oc * 128:(oc + 1) * 128].rearrange("(dt pp) c -> pp dt c",
                                                             pp=128)
        P.dma("gpsimd", WOS[:, :].rearrange("p (dt c) -> p dt c", dt=8), src,
              waits=P.w("pe"), inc="dw")
        wd = [("dw", P.n("dw"))]
        for tb in range(4):
            reg = PA[:, (tb % 2) * 512:(tb % 2) * 512 + 512]
            for dt in range(8):
                P.op("tensor", lambda e, dt=dt, tb=tb, reg=reg: e.matmul(
                    reg, WOS[:, dt * 128:(dt + 1) * 128],
                    Y[:, dt * 2048 + tb * 512: dt * 2048 + tb * 512 + 512],
                    start=(dt == 0), stop=(dt == 7)),
                    waits=(wd + P.w("ve") if dt == 0 else ()),
                    inc="pe" if dt == 7 else None)
            P.op("vector", lambda e, oc=oc, tb=tb, reg=reg: e.scalar_tensor_tensor(
                X[:, oc * 2048 + tb * 512: oc * 2048 + tb * 512 + 512], reg,
                s.PKO[:, 24 + oc:25 + oc],
                X[:, oc * 2048 + tb * 512: oc * 2048 + tb * 512 + 512],
                ALU.add, ALU.add), waits=P.w("pe"), inc="ve")

    if DBG_OUTER:
        P.dma("sync", s.dbg[3], X[:, :], waits=P.w("ve", "pe"))
    emit_feature_ln(P, s, src_shift=False, affine=None, out_xn=None)
    if DBG_OUTER:
        P.dma("sync", s.dbg[4], X[:, :], waits=P.w("ve"))
    P.dma("sync", s.uscr[:, :], X[:, :], waits=P.w("ve"))



def build_inner_attn(P, s, l):
    X, PA, PB, PC, PD, PSF = s.X, s.PA, s.PB, s.PC, s.PD, s.PSF
    arb, arf32 = s.arb, s.arf32
    XS = arb(0, 16384)
    AO = arb(16384, 8192)
    QP = arb(24576, 1024)
    KP = arb(25600, 1024)
    VF = arb(26624, 1024)
    VP1 = arb(27648, 1056)
    STGQ = arb(28704, 1088)
    STGK = arb(29792, 1280)
    KTAIL = [arb(31072 + h * 256, 256) for h in range(16)]
    VTAIL1 = [arb(35168 + p * 272, 272) for p in range(8)]
    SF1 = arf32(37440, 1024)
    SF2 = arf32(39488, 1024)
    AT = arb(41536, 384)
    YBF = arf32(41920, 64)
    NM = arf32(42048, 32)
    M16 = arf32(42112, 128)
    RD = arf32(42368, 16)
    WQS = [[arb(42400 + (b * 3 + i) * 1024, 1024) for i in range(3)] for b in range(2)]
    WOS = [arb(48544 + i * 1024, 1024) for i in range(2)]

    if TRUNC == -1:
        return

    def vp1(tile, hh):
        return VP1[:, tile * 130 + hh * 65: tile * 130 + (hh + 1) * 65]

    def vtail1(p, tile, hh):
        return VTAIL1[p][:, tile * 130 + hh * 65: tile * 130 + (hh + 1) * 65]

    P.dma("sync", s.PACK[:], s.packl_in[l], waits=P.w("ve", "se", "pe"))
    wpack = [("dx", P.n("dx"))]
    if TRUNC == 10:
        return

    for dt in range(8):
        dst = XS[:, dt * 2048:(dt + 1) * 2048]
        w0 = P.w("pe") if dt == 0 else ()
        if dt >= 4:
            P.op("vector", lambda e, dt=dt, dst=dst: e.tensor_copy(
                dst[:, 1:2048], X[:, dt * 2048: dt * 2048 + 2047]), waits=w0, inc="ve")
            P.op("vector", lambda e, dst=dst: e.memset(dst[:, 0:1], 0.0), inc="ve")
        else:
            P.op("vector", lambda e, dt=dt, dst=dst: e.tensor_copy(
                dst, X[:, dt * 2048:(dt + 1) * 2048]), waits=w0, inc="ve")
    if TRUNC == 11:
        return
    P.op("vector", lambda e: e.memset(STGK[64:65, :], 1.0), inc="ve")
    for h in range(16):
        P.op("vector", lambda e, h=h: e.memset(KTAIL[h][64:65, :], 1.0), inc="ve")
    for tile in range(8):
        for hh in range(2):
            c = tile * 130 + hh * 65 + 64
            P.op("vector", lambda e, c=c: e.memset(VP1[:, c:c + 1], 1.0), inc="ve")

    wxs = P.w("ve")
    if TRUNC == 12:
        return

    def slab_dma(hf, p, bufs, waits):
        for which in range(3):
            c0 = which * 1024 + p * 128
            src = s.wqkv[l][:, c0:c0 + 128].rearrange("(dt pp) c -> pp dt c", pp=128)
            P.dma("gpsimd", bufs[which][:, :].rearrange("p (dt c) -> p dt c", dt=8),
                  src, waits=(waits if which == 0 else ()), inc="dw")
        return P.n("dw")

    slab_ready = {}
    slab_ready[(0, 0)] = slab_dma(0, 0, WQS[0], wxs)

    if TRUNC == 0:
        return
    for hf in range(2):
        toff = hf * 1024
        for p in range(NPAIR):
            gidx = hf * NPAIR + p
            buf = WQS[gidx % 2]
            ni = gidx + 1
            if ni < 16:
                nhf, np_ = divmod(ni, NPAIR)
                slab_ready[(nhf, np_)] = slab_dma(nhf, np_, WQS[ni % 2], P.w("pe"))
            wsl = [("dw", slab_ready[(hf, p)])]

            def halfmm(slab, extra):
                for tb in range(2):
                    for dt in range(8):
                        P.op("tensor", lambda e, tb=tb, dt=dt, slab=slab, toff=toff:
                             e.matmul(
                                 PA[:, tb * 512:(tb + 1) * 512],
                                 slab[:, dt * 128:(dt + 1) * 128],
                                 XS[:, dt * 2048 + toff + tb * 512:
                                    dt * 2048 + toff + tb * 512 + 512],
                                 start=(dt == 0), stop=(dt == 7)),
                             waits=(extra if (tb == 0 and dt == 0) else ()),
                             inc="pe" if (tb == 1 and dt == 7) else None)

            def qk_pipeline(slab, qs_col, tab_c, tab_s, OUT, wfirst):
                halfmm(slab, wfirst)
                wmm = P.w("pe")
                P.op("scalar", lambda e: e.activation(VF[:, :], PA, AF.Square),
                     waits=wmm + P.w("ve"), inc="se")
                for tb in range(2):
                    P.op("tensor", lambda e, tb=tb: e.matmul(
                        PB[:, tb * 512:(tb + 1) * 512], s.BBLK[:],
                        VF[:, tb * 512:(tb + 1) * 512], start=True, stop=True),
                        waits=P.w("se", "ve"), inc="pe" if tb == 1 else None)
                P.op("vector", lambda e: e.tensor_scalar_max(SF1[:, :], PB, 1e-24),
                     waits=P.w("pe"), inc="ve")
                P.op("scalar", lambda e: e.activation(SF2[:, :], SF1[:, :], AF.Ln),
                     waits=P.w("ve"), inc="se")
                P.op("scalar", lambda e: e.activation(SF1[:, :], SF2[:, :], AF.Exp,
                                                      scale=-0.5), inc="se")
                P.op("vector", lambda e, qs_col=qs_col: e.scalar_tensor_tensor(
                    SF2[:, :], PA, s.PACK[:, qs_col:qs_col + 1], SF1[:, :],
                    ALU.mult, ALU.mult), waits=P.w("se") + wpack, inc="ve")
                P.op("vector", lambda e, OUT=OUT: e.tensor_copy(OUT[:, :], SF2[:, :]),
                     inc="ve")
                for tb in range(2):
                    P.op("tensor", lambda e, tb=tb, OUT=OUT: e.matmul(
                        PB[:, tb * 512:(tb + 1) * 512], s.PROT[:],
                        OUT[:, tb * 512:(tb + 1) * 512], start=True, stop=True),
                        waits=P.w("ve"), inc="pe" if tb == 1 else None)
                P.op("vector", lambda e, tab_c=tab_c, toff=toff: e.tensor_mul(
                    SF1[:, :], SF2[:, :],
                    s.TAB[:, tab_c * T + toff: tab_c * T + toff + 1024]), inc="ve")
                P.op("vector", lambda e, tab_s=tab_s, toff=toff: e.tensor_mul(
                    SF2[:, :], PB, s.TAB[:, tab_s * T + toff: tab_s * T + toff + 1024]),
                    waits=P.w("pe"), inc="ve")
                P.op("vector", lambda e, OUT=OUT: e.tensor_add(OUT[:, :], SF1[:, :],
                                                               SF2[:, :]), inc="ve")

            if TRUNC == 1:
                halfmm(buf[0], wsl + P.w("ve", "se"))
                continue
            qk_pipeline(buf[0], 0, 0, 1, QP, wsl + P.w("ve", "se"))
            if DBG_INNER and l == 0 and hf == 0 and p == 0:
                P.dma("gpsimd", s.dbg[DEBUG_TAPS][:, 2048:3072], VF[:, :],
                      waits=P.w("se", "ve"), inc="dw")
                P.dma("sync", s.dbg[DEBUG_TAPS][:, 3072:4096],
                      s.arf32(37440, 1024)[:, :], waits=P.w("se", "ve"))
                P.dma("sync", s.dbg[DEBUG_TAPS][:, 4096:5120],
                      s.arf32(39488, 1024)[:, :], waits=P.w("se", "ve"))
            qk_pipeline(buf[1], 1, 2, 3, KP, P.w("ve", "se") +
                        ([("dx", P.n("dx")), ("dw", P.n("dw"))]
                         if (DBG_INNER and l == 0 and hf == 0 and p == 0) else []))
            if DBG_INNER and l == 0 and hf == 0 and p == 0:
                P.dma("gpsimd", s.dbg[DEBUG_TAPS][:, 0:1024], QP[:, :],
                      waits=P.w("ve"), inc="dw")
                P.dma("gpsimd", s.dbg[DEBUG_TAPS][:, 1024:2048], KP[:, :],
                      waits=P.w("ve"), inc="dw")

            if TRUNC == 13:
                continue
            halfmm(buf[2], P.w("ve", "se"))
            P.op("vector", lambda e: e.tensor_copy(SF1[:, :], PA), waits=P.w("pe"),
                 inc="ve")
            for tt in range(8):
                pd = PD[:, 0:128] if tt % 2 == 0 else PD[:, 128:256]
                P.op("tensor", lambda e, tt=tt, pd=pd: e.transpose(
                    pd, SF1[:, tt * 128:(tt + 1) * 128], s.IDF[:]),
                    waits=P.w("ve"), inc="pe")
                P.op("vector", lambda e, tt=tt, pd=pd: e.tensor_copy(
                    vp1(tt, 0)[:, 0:64], pd[:, 0:64]), waits=P.w("pe"), inc="ve")
                P.op("vector", lambda e, tt=tt, pd=pd: e.tensor_copy(
                    vp1(tt, 1)[:, 0:64], pd[:, 64:128]), inc="ve")
            if hf == 0:
                P.op("vector", lambda e, p=p: e.tensor_copy(
                    VTAIL1[p][:, 0:260], VP1[:, 780:1040]), inc="ve")

            if TRUNC == 2:
                continue
            for hh in range(2):
                h = 2 * p + hh
                hb = hh * 64
                P.op("vector", lambda e, hb=hb: e.tensor_scalar_mul(
                    STGQ[0:64, 0:1024], QP[hb:hb + 64, :], 8.0),
                    waits=P.w("pe"), inc="ve")
                P.op("vector", lambda e, hb=hb: e.tensor_copy(STGK[0:64, 256:1280],
                                                              KP[hb:hb + 64, :]),
                     inc="ve")
                if hf == 0:
                    P.op("vector", lambda e, h=h: e.tensor_copy(
                        KTAIL[h][0:65, :], STGK[0:65, 1024:1280]), inc="ve")
                else:
                    P.op("vector", lambda e, h=h: e.tensor_copy(
                        STGK[0:65, 0:256], KTAIL[h][0:65, :]), inc="ve")
                wstg = P.w("ve")

                def kwin(qb_g, w, rows):
                    hi = (qb_g * 128 + 128 - hf * 1024) + 256
                    return STGK[0:rows, hi - w:hi]

                def vtile_src(gjt, hh):
                    if gjt < hf * 8:
                        return vtail1(p, gjt - (hf * 8 - 2), hh)
                    return vp1(gjt - hf * 8, hh)

                if TRUNC == 15:
                    continue
                for qb in range(8):
                    qb_g = hf * 8 + qb
                    w = 128 if qb_g == 0 else (256 if qb_g == 1 else 384)
                    reg = PC[:, 0:384] if qb % 2 == 0 else PC[:, 512:896]
                    kap = kwin(qb_g, w, 64)
                    P.op("tensor", lambda e, qb=qb, w=w, reg=reg, kap=kap: e.matmul(
                        reg[:, 0:w], STGQ[0:64, qb * 128:(qb + 1) * 128],
                        kap, start=True, stop=True),
                        waits=wstg + P.w("ve"), inc="pe")
                    P.op("vector", lambda e, w=w, reg=reg: e.tensor_add(
                        SF1[:, 0:w], reg[:, 0:w], s.MASKI[:, 384 - w:384]),
                        waits=P.w("pe"), inc="ve")
                    P.op("vector", lambda e, qb=qb, w=w: e.reduce_max(
                        NM[:, qb:qb + 1], SF1[:, 0:w], axis=AX.X, negate=True),
                        inc="ve")
                P.op("tensor", lambda e: e.transpose(PC[0:8, 0:128], NM[:, 0:8],
                                                     s.IDF[:]),
                     waits=P.w("ve"), inc="pe")
                P.op("vector", lambda e: e.tensor_copy(M16[0:8, 0:128], PC[0:8, 0:128]),
                     waits=P.w("pe", "dm"), inc="ve")
                P.dma("gpsimd", STGQ[64:65, 0:1024], M16[0:8, 0:128],
                      waits=P.w("ve"), inc="dm")
                wmr = [("dm", P.n("dm"))]
                if TRUNC == 16:
                    continue
                if DBG_INNER and l == 0 and hf == 0 and h == 0:
                    P.dma("gpsimd", s.dbg[DEBUG_TAPS + 1][:, 0:1088],
                          STGQ[:, 0:1088], waits=wmr, inc="dw")
                    P.dma("gpsimd", s.dbg[DEBUG_TAPS + 1][:, 1088:2368],
                          STGK[:, 0:1280], waits=wmr, inc="dw")
                    P.dma("sync", s.dbg[DEBUG_TAPS + 1][:, 2368:2400],
                          NM[:, 0:32], waits=wmr)
                    P.dma("sync", s.dbg[DEBUG_TAPS + 1][:, 2432:2560],
                          M16[:, 0:128], waits=wmr)

                for qb in range(8):
                    qb_g = hf * 8 + qb
                    w = 128 if qb_g == 0 else (256 if qb_g == 1 else 384)
                    jlo_g = max(0, qb_g * 128 - 256)
                    ct = w // 128
                    reg = PC[:, 0:384] if qb % 2 == 0 else PC[:, 512:896]
                    for c in range(ct):
                        lo = (jlo_g + c * 128 - hf * 1024) + 256
                        P.op("tensor", lambda e, qb=qb, c=c, lo=lo, ct=ct, reg=reg:
                             e.matmul(
                                 reg[:, c * 128:(c + 1) * 128],
                                 STGK[0:65, lo:lo + 128],
                                 STGQ[0:65, qb * 128:(qb + 1) * 128],
                                 start=True, stop=True),
                             waits=(wmr + P.w("se", "ve") if c == 0 else ()),
                             inc="pe" if c == ct - 1 else None)
                    P.op("vector", lambda e, w=w, reg=reg: e.tensor_add(
                        SF2[:, 0:w], reg[:, 0:w], s.MASKT[:, 384 - w:384]),
                        waits=P.w("pe"), inc="ve")
                    P.op("scalar", lambda e, w=w: e.activation(
                        AT[:, 0:w], SF2[:, 0:w], AF.Exp), waits=P.w("ve"), inc="se")
                    for c in range(ct):
                        gjt = jlo_g // 128 + c
                        vsrc = vtile_src(gjt, hh)
                        P.op("tensor", lambda e, c=c, ct=ct, vsrc=vsrc: e.matmul(
                            PC[:, 896:961], AT[:, c * 128:(c + 1) * 128], vsrc,
                            start=(c == 0), stop=(c == ct - 1)),
                            waits=(P.w("se", "ve") if c == 0 else ()),
                            inc="pe" if c == ct - 1 else None)
                    P.op("vector", lambda e: e.reciprocal(RD[:, 0:1], PC[:, 960:961]),
                         waits=P.w("pe"), inc="ve")
                    P.op("vector", lambda e: e.tensor_single_scalar(
                        YBF[:, 0:64], PC[:, 896:960], RD[:, 0:1], ALU.mult),
                        waits=P.w("ve"), inc="ve")
                    ypd = PD[:, 256:384] if qb % 2 == 0 else PD[:, 384:512]
                    P.op("tensor", lambda e, ypd=ypd: e.transpose(
                        ypd[0:64, :], YBF[:, 0:64], s.IDF[:]), waits=P.w("ve"),
                        inc="pe")
                    dt = h // 2
                    rb = (h % 2) * 64
                    P.op("vector", lambda e, dt=dt, rb=rb, qb=qb, ypd=ypd: e.tensor_copy(
                        AO[rb:rb + 64, dt * 1024 + qb * 128: dt * 1024 + qb * 128 + 128],
                        ypd[0:64, :]), waits=P.w("pe"), inc="ve")

        if DBG_INNER and l == 0:
            P.dma("gpsimd", s.dbg[DEBUG_TAPS + 2][:, hf * 8192:hf * 8192 + 8192],
                  AO[:, :], waits=P.w("ve", "pe"), inc="dw")
        if TRUNC == 3:
            continue
        for oc in range(8):
            buf = WOS[oc % 2]
            src = s.wo[l][:, oc * 128:(oc + 1) * 128].rearrange(
                "(dt pp) c -> pp dt c", pp=128)
            P.dma("gpsimd", buf[:, :].rearrange("p (dt c) -> p dt c", dt=8), src,
                  waits=(P.w("pe") if oc >= 2 else P.w("ve")), inc="dw")
            wd = [("dw", P.n("dw"))]
            for tb in range(2):
                reg = PA[:, tb * 512:(tb + 1) * 512]
                for dt in range(8):
                    P.op("tensor", lambda e, dt=dt, tb=tb, buf=buf, reg=reg, toff=toff:
                         e.matmul(
                             reg, buf[:, dt * 128:(dt + 1) * 128],
                             AO[:, dt * 1024 + tb * 512: dt * 1024 + tb * 512 + 512],
                             start=(dt == 0), stop=(dt == 7)),
                         waits=(wd + P.w("ve") if dt == 0 else ()),
                         inc="pe" if dt == 7 else None)
                c0 = oc * 2048 + toff + tb * 512
                P.op("vector", lambda e, c0=c0, reg=reg: e.tensor_add(
                    X[:, c0:c0 + 512], reg, X[:, c0:c0 + 512]),
                    waits=P.w("pe", "dx"), inc="ve")

def build_inner_ffn(P, s, l):
    X, PA, PB, PC, PD, PSF = s.X, s.PA, s.PB, s.PC, s.PD, s.PSF
    arb, arf32 = s.arb, s.arf32
    XN = arb(0, 16384)
    W1S = [[arb(16384 + (b * 2 + i) * 1024, 1024) for i in range(2)] for b in range(2)]
    H = [arb(20480 + m * 1024, 1024) for m in range(NMT)]
    SF1 = arf32(43008, 1024)
    SF2 = arf32(45056, 1024)
    GL = arb(47104, 1024)
    W2S = [arb(48128 + i * 1024, 1024) for i in range(2)]

    if DBG_INNER and l == 0:
        P.dma("sync", s.dbg[DEBUG_TAPS + 3], s.X[:, :], waits=P.w("ve", "pe", "se"))
    emit_feature_ln(P, s, src_shift=True, affine=(2, 10), out_xn=0)
    if DBG_INNER and l == 0:
        P.dma("gpsimd", s.dbg[DEBUG_TAPS + 4], XN[:, :], waits=P.w("ve", "pe", "se"),
              inc="dw")

    def w1_slab_dma(hfm, bufs, waits):
        hf, m = hfm
        mw = 128 if m < 21 else LAST_M
        for i, cbase in enumerate((0, DI)):
            c0 = cbase + m * 128
            src = s.w1[l][:, c0:c0 + mw].rearrange("(dt pp) c -> pp dt c", pp=128)
            P.dma("gpsimd", bufs[i][:, 0:8 * mw].rearrange("p (dt c) -> p dt c", dt=8),
                  src, waits=(waits if i == 0 else ()), inc="dw")
        return P.n("dw")

    w1_ready = {}
    w1_ready[(0, 0)] = w1_slab_dma((0, 0), W1S[0], P.w("ve", "pe"))

    for hf in range(2):
        toff = hf * 1024
        for m in range(NMT):
            gi = hf * NMT + m
            mw = 128 if m < 21 else LAST_M
            buf = W1S[gi % 2]
            ni = gi + 1
            if ni < 2 * NMT:
                nhf, nm = divmod(ni, NMT)
                w1_ready[(nhf, nm)] = w1_slab_dma((nhf, nm), W1S[ni % 2], P.w("pe"))
            wsl = [("dw", w1_ready[(hf, m)])]
            for tb in range(2):
                for which in range(2):
                    ps = PA if which == 0 else PB
                    for dt in range(8):
                        P.op("tensor",
                             lambda e, tb=tb, dt=dt, mw=mw, buf=buf, which=which,
                             ps=ps, toff=toff:
                             e.matmul(
                                 ps[0:mw, tb * 512:(tb + 1) * 512],
                                 buf[which][:, dt * mw:(dt + 1) * mw],
                                 XN[:, dt * 2048 + toff + tb * 512:
                                    dt * 2048 + toff + tb * 512 + 512],
                                 start=(dt == 0), stop=(dt == 7)),
                             waits=(wsl + P.w("se", "ve")
                                    if (tb == 0 and which == 0 and dt == 0) else ()),
                             inc="pe" if dt == 7 else None)
            for tb in range(2):
                P.op("scalar", lambda e, tb=tb, mw=mw, m=m: e.activation(
                    GL[0:mw, tb * 512:(tb + 1) * 512],
                    PB[0:mw, tb * 512:(tb + 1) * 512], AF.Gelu,
                    bias=s.PACK[0:mw, 48 + m:49 + m]), waits=P.w("pe"), inc="se")
                P.op("vector", lambda e, tb=tb, mw=mw, m=m: e.scalar_tensor_tensor(
                    H[m][0:mw, tb * 512:(tb + 1) * 512],
                    PA[0:mw, tb * 512:(tb + 1) * 512],
                    s.PACK[0:mw, 26 + m:27 + m],
                    GL[0:mw, tb * 512:(tb + 1) * 512],
                    ALU.add, ALU.mult), waits=P.w("se"), inc="ve")

        if DBG_INNER and l == 0 and hf == 0:
            P.dma("gpsimd", s.dbg[DEBUG_TAPS + 1][:, 5760:6784], H[0][:, :],
                  waits=P.w("ve", "pe"), inc="dw")
            P.dma("gpsimd", s.dbg[DEBUG_TAPS + 1][:, 6784:7808], H[21][:, :],
                  waits=P.w("ve", "pe"), inc="dw")
            P.dma("gpsimd", s.dbg[DEBUG_TAPS + 1][:, 7808:8832], H[10][:, :],
                  waits=P.w("ve", "pe"), inc="dw")
        def w2_slab_dma(m, buf, waits):
            mw = 128 if m < 21 else LAST_M
            P.dma("gpsimd", buf[0:mw, :], s.w2[l][m * 128:m * 128 + mw, :],
                  waits=waits, inc="dw")
            return P.n("dw")

        for qq in range(2):
            w2_ready = {0: w2_slab_dma(0, W2S[0], P.w("pe", "ve"))}
            for m in range(NMT):
                mw = 128 if m < 21 else LAST_M
                buf = W2S[m % 2]
                if m + 1 < NMT:
                    w2_ready[m + 1] = w2_slab_dma(m + 1, W2S[(m + 1) % 2], P.w("pe"))
                wsl = [("dw", w2_ready[m])]
                for oc in range(8):
                    ps = PSF[:, oc * 512:(oc + 1) * 512]
                    P.op("tensor", lambda e, m=m, mw=mw, oc=oc, qq=qq, buf=buf, ps=ps:
                         e.matmul(
                             ps, buf[0:mw, oc * 128:(oc + 1) * 128],
                             H[m][0:mw, qq * 512:(qq + 1) * 512],
                             start=(m == 0), stop=(m == NMT - 1)),
                         waits=(wsl + (P.w("ve", "se") if (m == 0 and oc == 0)
                                       else []) if oc == 0 else ()),
                         inc="pe" if oc == 7 else None)
            wmm = P.w("pe")
            for oc in range(8):
                ps = PSF[:, oc * 512:(oc + 1) * 512]
                c0 = oc * 2048 + toff + qq * 512
                P.op("vector", lambda e, oc=oc, c0=c0, ps=ps: e.scalar_tensor_tensor(
                    X[:, c0:c0 + 512], ps, s.PACK[:, 18 + oc:19 + oc],
                    X[:, c0:c0 + 512], ALU.add, ALU.add), waits=wmm, inc="ve")



def build_final(P, s):
    X, PA, PB = s.X, s.PA, s.PB
    arb, arf32 = s.arb, s.arf32
    UF = arf32(16384, 16384)
    OT = [arf32(0, 1024), arf32(4096, 1024)]
    P.dma("sync", UF[:, :], s.uscr[:, :], waits=P.w("ve", "pe", "se"))
    wu = P.w("dx")
    for dt in range(8):
        P.op("vector", lambda e, dt=dt: e.tensor_add(
            X[:, dt * 2048:(dt + 1) * 2048], X[:, dt * 2048:(dt + 1) * 2048],
            UF[:, dt * 2048:(dt + 1) * 2048]), waits=(wu if dt == 0 else ()), inc="ve")
    emit_feature_ln(P, s, src_shift=False, affine=None, out_xn=None)
    for tt in range(16):
        ot = OT[tt % 2]
        for dt in range(8):
            par = dt % 2
            ps = PA[:, 0:128] if par == 0 else PB[:, 0:128]
            P.op("tensor", lambda e, tt=tt, dt=dt, ps=ps: e.transpose(
                ps, X[:, dt * 2048 + tt * 128: dt * 2048 + tt * 128 + 128], s.IDF[:]),
                waits=P.w("ve"), inc="pe")
            wfree = P.w("pe") + ([("dx", P.n("dx"))] if (dt == 0 and tt >= 2) else [])
            P.op("vector", lambda e, dt=dt, ps=ps, ot=ot: e.tensor_copy(
                ot[:, dt * 128:(dt + 1) * 128], ps), waits=wfree, inc="ve")
        P.dma("sync", s.y_out[tt * 128:(tt + 1) * 128, :], ot[:, :], waits=P.w("ve"))



def kernel(**inputs):
    tabs = _host_tables()
    maski, maskt = _host_masks()
    idf, idb, bblk, prot, ltrilt, onesr = _host_consts()
    pko, packl = _host_packs(inputs)

    nc = build_program()

    x = np.ascontiguousarray(np.asarray(inputs["L0_attn_x"], np.float32))
    shared = dict(
        w_attn=np.ascontiguousarray(np.asarray(inputs["L0_attn_W_attn"], np.float32)),
        w_proj=np.ascontiguousarray(np.asarray(inputs["L0_attn_W_proj"], np.float32)),
        wqkv=np.ascontiguousarray(np.asarray(inputs["L0_ffn_Wqkv"], np.float32)),
        wo=np.ascontiguousarray(np.asarray(inputs["L0_ffn_Wo"], np.float32)),
        w1=np.ascontiguousarray(np.asarray(inputs["L0_ffn_W1"], np.float32)),
        w2=np.ascontiguousarray(np.asarray(inputs["L0_ffn_W2"], np.float32)),
        tabs=tabs, maski=maski, maskt=maskt, idf=idf, idb=idb, bblk=bblk,
        prot=prot, ltrilt=ltrilt, onesr=onesr, pko=pko, packl=packl,
        nidf=-idf,
    )
    in_maps = [dict(shared, x=np.ascontiguousarray(x[b])) for b in range(8)]
    res = run_bass_kernel_spmd(nc, in_maps, core_ids=list(range(8)))
    globals()["LAST_RES"] = res
    out = np.stack([np.asarray(res.results[b]["y"]) for b in range(8)], axis=0)
    return out


if __name__ == "__main__":
    nc = build_program()
    print("build ok")
